# version 01
# baseline (speedup 1.0000x reference)
from contextlib import ExitStack
import numpy as np
import concourse.bass as bass
import concourse.mybir as mybir
from concourse.alu_op_type import AluOpType as OP

F32 = mybir.dt.float32
BF16 = mybir.dt.bfloat16
AF = mybir.ActivationFunctionType

D = 1024
NTOK = 2048
NCH = 8
NTILE = 16
TB = 512
NBLK = NTOK // TB

def tmap(half):
    return [t for t in range(32) if ((t % 4) in (0, 3)) == (half == 0)]

def own_token_idx(half):
    return np.concatenate([np.arange(t * 128, (t + 1) * 128) for t in tmap(half)])

ALIBI_SLOPES = np.asarray([2.0 ** (-(i + 1) / 2.0) for i in range(16)], np.float32)


class _Handle:
    def __init__(self):
        self.incs = []
    def then_inc(self, sem, n=1):
        self.incs.append((sem, n))
        return self


class _EngProxy:
    def __init__(self, plan, eng):
        self._plan = plan
        self._eng = eng
    def __getattr__(self, name):
        def rec(*args, **kwargs):
            h = _Handle()
            self._plan.ops[self._eng].append((name, args, kwargs, h))
            return h
        return rec


class Plan:
    ENGINES = ("sync", "gpsimd", "tensor", "vector", "scalar")
    def __init__(self, nc):
        self.nc = nc
        self.ops = {e: [] for e in self.ENGINES}
        self._vsem = nc.semaphore("vchain").__enter__()
        self._vscr = nc.sbuf_tensor("vscratch", [1, 2], mybir.dt.float32).__enter__()
        self.t = _EngProxy(self, "tensor")
        self.v = _EngProxy(self, "vector")
        self.s = _EngProxy(self, "scalar")
        self.g = _EngProxy(self, "gpsimd")
        self.y = _EngProxy(self, "sync")

    def build(self):
        nc = self.nc
        vsem = self._vsem
        vscr = self._vscr
        with nc.Block() as block:
            for eng in self.ENGINES:
                lst = self.ops[eng]
                if not lst:
                    continue
                def make(lst, eng=eng):
                    def body(e):
                        cnt = 0
                        for name, args, kwargs, h in lst:
                            if eng != "vector" or name == "wait_ge":
                                r = getattr(e, name)(*args, **kwargs)
                                for sm, n in h.incs:
                                    r = r.then_inc(sm, n)
                                continue
                            if cnt > 0:
                                e.wait_ge(vsem, cnt)
                            r = getattr(e, name)(*args, **kwargs)
                            r.then_inc(vsem, 1)
                            cnt += 1
                            for sm, n in h.incs:
                                e.wait_ge(vsem, cnt)
                                e.tensor_copy(vscr[:, 0:1], vscr[:, 1:2])\
                                    .then_inc(sm, n)
                                cnt += 0
                    return body
                getattr(block, eng)(make(lst))


def sb(st, nc, name, shape, dtype, side=None):
    kw = {"side": side} if side else {}
    return st.enter_context(nc.sbuf_tensor(name, shape, dtype, **kw))

def ps(st, nc, name, shape, dtype=F32):
    return st.enter_context(nc.psum_tensor(name, shape, dtype))

def sem(st, nc, name):
    return st.enter_context(nc.semaphore(name))

def cols(blk, tb=TB):
    return slice(blk * tb, (blk + 1) * tb)


def emit_ln(p, stm, sts, name, src, dst_f32=None, dst_bf=None, gamma=None, beta=None):
    nc = p.nc
    sq = [sb(stm, nc, f"{name}_sq{i}", [128, NTOK], BF16, side="right") for i in range(2)]
    xbf = [sb(stm, nc, f"{name}_xbf{i}", [128, NTOK], BF16, side="right") for i in range(2)]
    srow = [sb(stm, nc, f"{name}_sr{i}", [1, TB], F32, side="right") for i in range(8)]
    R = [sb(stm, nc, f"{name}_R{i}", [1, TB], F32, side="right") for i in range(5)]
    tmp = [sb(stm, nc, f"{name}_tmp{i}", [128, TB], F32, side="right") for i in range(2)]
    ones_bf = sb(stm, nc, f"{name}_1bf", [128, 1], BF16, side="right")
    ones_f = sb(stm, nc, f"{name}_1f", [1, 128], F32, side="right")
    s_pre = sem(sts, nc, f"{name}_pre")
    s_mm = sem(sts, nc, f"{name}_mm")
    s_row = sem(sts, nc, f"{name}_row")
    s_nrm = sem(sts, nc, f"{name}_nrm")

    p.v.memset(ones_bf[:], 1.0)
    p.v.memset(ones_f[:], 1.0)
    pst_scope = ExitStack()
    ps_st = [ps(pst_scope, nc, f"{name}_ps{i}", [1, TB]) for i in range(8)]
    for d in range(NCH):
        if d >= 2:
            p.s.wait_ge(s_mm, d - 1)
            p.v.wait_ge(s_mm, d - 1)
        p.s.activation(sq[d % 2][:], src[d], AF.Square).then_inc(s_pre, 1)
        p.v.tensor_copy(xbf[d % 2][:], src[d]).then_inc(s_pre, 1)
        p.t.wait_ge(s_pre, 2 * d + 2)
        for blk in range(NBLK):
            p.t.matmul(ps_st[blk][:], ones_bf[:], xbf[d % 2][:, cols(blk)],
                       start=(d == 0), stop=(d == NCH - 1))
        for blk in range(NBLK):
            h = p.t.matmul(ps_st[4 + blk][:], ones_bf[:], sq[d % 2][:, cols(blk)],
                           start=(d == 0), stop=(d == NCH - 1))
        h.then_inc(s_mm, 1)

    p.v.wait_ge(s_mm, NCH)
    for blk in range(NBLK):
        p.v.tensor_copy(srow[blk][:], ps_st[blk][:])
        h = p.v.tensor_copy(srow[4 + blk][:], ps_st[4 + blk][:])
    h.then_inc(s_row, 1)
    pst_scope.close()
    ps_b = [ps(stm, nc, f"{name}_pb{i}", [128, TB]) for i in range(4)]

    for blk in range(NBLK):
        base = 1 + 5 * blk
        p.v.tensor_scalar(R[0][:], srow[blk][:], 1.0 / D, None, OP.mult)
        p.v.tensor_scalar(R[1][:], srow[4 + blk][:], 1.0 / D, None, OP.mult)\
            .then_inc(s_row, 1)
        p.s.wait_ge(s_row, base + 1)
        p.s.activation(R[2][:], R[0][:], AF.Square).then_inc(s_row, 1)
        p.v.wait_ge(s_row, base + 2)
        p.v.scalar_tensor_tensor(R[1][:], R[1][:], 1e-5, R[2][:], OP.add, OP.subtract)\
            .then_inc(s_row, 1)
        p.s.wait_ge(s_row, base + 3)
        p.s.activation(R[2][:], R[1][:], AF.Sqrt).then_inc(s_row, 1)
        p.v.wait_ge(s_row, base + 4)
        p.v.reciprocal(R[3][:], R[2][:])
        p.v.tensor_tensor(R[4][:], R[0][:], R[3][:], OP.mult)\
            .then_inc(s_row, 1)
        br, bm = ps_b[2 * (blk % 2)], ps_b[2 * (blk % 2) + 1]
        p.t.wait_ge(s_row, base + 5)
        if blk >= 2:
            p.t.wait_ge(s_nrm, (blk - 1) * NCH)
        p.t.matmul(br[:], ones_f[:], R[3][:], start=True, stop=True)
        p.t.matmul(bm[:], ones_f[:], R[4][:], start=True, stop=True)\
            .then_inc(s_mm, 1)
        p.v.wait_ge(s_mm, NCH + 1 + blk)
        for d in range(NCH):
            t0, t1 = tmp[0], tmp[1]
            p.v.tensor_tensor(t0[:], src[d][:, cols(blk)], br[:], OP.mult)
            hxn = p.v.tensor_tensor(t1[:], t0[:], bm[:], OP.subtract)
            if gamma is not None:
                if dst_f32 is not None:
                    p.v.tensor_scalar(dst_f32[d][:, cols(blk)], t1[:],
                                      gamma[d], beta[d], OP.mult, OP.add)
                if dst_bf is not None:
                    hxn = p.v.tensor_scalar(dst_bf[d][:, cols(blk)], t1[:],
                                            gamma[d], beta[d], OP.mult, OP.add)
            else:
                if dst_f32 is not None:
                    hxn = p.v.tensor_copy(dst_f32[d][:, cols(blk)], t1[:])
                if dst_bf is not None:
                    hxn = p.v.tensor_copy(dst_bf[d][:, cols(blk)], t1[:])
            hxn.then_inc(s_nrm, 1)
    return s_nrm, NBLK * NCH


class StagedEvac:
    def __init__(self, p, stm, sts, name, n_stages=4, tb=TB, dtype=BF16):
        self.p = p
        self.stages = [sb(stm, p.nc, f"{name}_stg{i}", [128, tb], dtype)
                       for i in range(n_stages)]
        self.n_stages = n_stages
        self.s_st = sem(sts, p.nc, f"{name}_sst")
        self.s_dma = sem(sts, p.nc, f"{name}_sdma")
        self.g = 0

    def __call__(self, bank_ap, s_mm, mmcnt, dst_ap, scale=1.0, bias=0.0):
        p = self.p
        g = self.g
        stage = self.stages[g % self.n_stages]
        p.s.wait_ge(s_mm, mmcnt)
        if g >= self.n_stages:
            p.s.wait_ge(self.s_dma, 16 * (g - self.n_stages + 1))
        func = AF.Copy if isinstance(bias, float) else AF.Identity
        h = p.s.activation(stage[:], bank_ap, func, scale=scale, bias=bias)
        h.then_inc(self.s_st, 1)
        p.y.wait_ge(self.s_st, g + 1)
        p.y.dma_start(dst_ap, stage[:]).then_inc(self.s_dma, 16)
        self.g += 1
        return (self.s_st, self.g)

    def flush_wait(self):
        self.p.y.wait_ge(self.s_dma, 16 * self.g)


def emit_fm_matmul(p, stm, sts, name, oc_list, nch, get_lhsT, get_rhs, evac,
                   nblk=NBLK, tb=TB):
    nc = p.nc
    pss = [ps(stm, nc, f"{name}_ps{i}", [128, tb]) for i in range(4)]
    s_mm = sem(sts, nc, f"{name}_mm")
    frees = []
    g = 0
    for oc in oc_list:
        for blk in range(nblk):
            bank = pss[g % 4]
            if g >= 4:
                fs, fv = frees[g - 4]
                p.t.wait_ge(fs, fv)
            for d in range(nch):
                h = p.t.matmul(bank[:], get_lhsT(d, oc), get_rhs(d, blk),
                               start=(d == 0), stop=(d == nch - 1))
            h.then_inc(s_mm, 1)
            frees.append(evac(oc, blk, bank, s_mm, g + 1))
            g += 1
    return frees[-1]


def build_prep(layer):
    nc = bass.Bass("TRN2", target_bir_lowering=False)
    NE = 14 if layer == 0 else 16
    xT = nc.dram_tensor("xT", [NCH, 128, NTOK], F32, kind="ExternalInput").ap()
    wqkvT = nc.dram_tensor("wqkvT", [NCH, 128, 3072], F32, kind="ExternalInput").ap()
    gwT = nc.dram_tensor("gwT", [NCH, 128, NE], F32, kind="ExternalInput").ap()
    ident = nc.dram_tensor("ident", [128, 128], F32, kind="ExternalInput").ap()
    if layer == 0:
        gcol = nc.dram_tensor("gcol", [NCH, 128, 1], F32, kind="ExternalInput").ap()
        bcol = nc.dram_tensor("bcol", [NCH, 128, 1], F32, kind="ExternalInput").ap()
    else:
        bgc = nc.dram_tensor("bgc", [NE, 1], F32, kind="ExternalInput").ap()
    o_xn = nc.dram_tensor("o_xn", [NCH, 128, NTOK], F32, kind="ExternalOutput").ap()
    o_xnbf = nc.dram_tensor("o_xnbf", [NCH, 128, NTOK], BF16, kind="ExternalOutput").ap()
    o_q = nc.dram_tensor("o_q", [NCH, 128, NTOK], BF16, kind="ExternalOutput").ap()
    o_k = nc.dram_tensor("o_k", [NCH, 128, NTOK], BF16, kind="ExternalOutput").ap()
    o_v = nc.dram_tensor("o_v", [NTILE, 128, D], BF16, kind="ExternalOutput").ap()
    o_gate = nc.dram_tensor("o_gate", [NE, NTOK], F32, kind="ExternalOutput").ap()

    p = Plan(nc)
    st = ExitStack()
    x_scope = ExitStack()
    x_sb = [sb(x_scope, nc, f"x{d}", [128, NTOK], F32, side="right") for d in range(NCH)]
    xn_sb = [sb(st, nc, f"xn{d}", [128, NTOK], F32) for d in range(NCH)]
    xnbf_sb = [sb(st, nc, f"xnbf{d}", [128, NTOK], BF16) for d in range(NCH)]
    id_sb = sb(st, nc, "idf", [128, 128], F32)
    s_ld = sem(st, nc, "ld")
    s_fin = sem(st, nc, "fin")
    fin = 0
    ld = 0

    for d in range(NCH):
        p.y.dma_start(x_sb[d][:], xT[d]).then_inc(s_ld, 16)
        ld += 16
    p.g.dma_start(id_sb[:], ident).then_inc(s_ld, 16)
    ld += 16
    if layer == 0:
        gc_sb = [sb(st, nc, f"gc{d}", [128, 1], F32) for d in range(NCH)]
        bc_sb = [sb(st, nc, f"bc{d}", [128, 1], F32) for d in range(NCH)]
        for d in range(NCH):
            p.g.dma_start(gc_sb[d][:], gcol[d]).then_inc(s_ld, 16)
            p.g.dma_start(bc_sb[d][:], bcol[d]).then_inc(s_ld, 16)
            ld += 32
    else:
        bg_sb = sb(st, nc, "bgsb", [NE, 1], F32)
        p.g.dma_start(bg_sb[:], bgc).then_inc(s_ld, 16)
        ld += 16
    LD_ALL = ld

    for e in (p.s, p.v, p.t):
        e.wait_ge(s_ld, LD_ALL)

    ln1_scope = ExitStack()
    s_n1, n1 = emit_ln(p, ln1_scope, st, "ln1", [x_sb[d][:] for d in range(NCH)],
                       dst_f32=xn_sb, dst_bf=xnbf_sb)
    p.y.wait_ge(s_n1, n1)
    for d in range(NCH):
        p.y.dma_start(o_xn[d], xn_sb[d][:]).then_inc(s_fin, 16)
        p.y.dma_start(o_xnbf[d], xnbf_sb[d][:]).then_inc(s_fin, 16)
        fin += 32

    if layer == 0:
        for e in (p.s, p.v, p.t):
            e.wait_ge(s_n1, n1)
        ln1_scope.close()
        x_scope.close()
        xn2bf_sb = [sb(st, nc, f"xn2bf{d}", [128, NTOK], BF16) for d in range(NCH)]
        ln2_scope = ExitStack()
        s_n2, n2 = emit_ln(p, ln2_scope, st, "ln2", [xn_sb[d][:] for d in range(NCH)],
                           dst_bf=xn2bf_sb, gamma=[gc_sb[d][:] for d in range(NCH)],
                           beta=[bc_sb[d][:] for d in range(NCH)])
        qkv_src = xn2bf_sb
        s_rdy, nrdy = s_n2, n2
        end_scope = ln2_scope
    else:
        qkv_src = xnbf_sb
        s_rdy, nrdy = s_n1, n1
        end_scope = ln1_scope

    for e in (p.s, p.v, p.t):
        e.wait_ge(s_rdy, nrdy)
    end_scope.close()
    if layer != 0:
        x_scope.close()

    w_sb = [sb(st, nc, f"w{d}", [128, 3072], BF16) for d in range(NCH)]
    gw_sb = [sb(st, nc, f"gw{d}", [128, NE], F32) for d in range(NCH)]
    p.g.wait_ge(s_rdy, nrdy)
    for d in range(NCH):
        p.g.dma_start(w_sb[d][:], wqkvT[d]).then_inc(s_ld, 16)
        p.g.dma_start(gw_sb[d][:], gwT[d]).then_inc(s_ld, 16)
        ld += 32
    LD_W = ld
    for e in (p.s, p.v, p.t):
        e.wait_ge(s_ld, LD_W)

    qkv_scope = ExitStack()
    ev_qk = StagedEvac(p, qkv_scope, st, "eqk")

    def evac_qk(oc, blk, bank, s_mm, mmcnt):
        dst = o_q if oc < 8 else o_k
        return ev_qk(bank[:], s_mm, mmcnt, dst[oc % 8, :, cols(blk)],
                     scale=(0.125 if oc < 8 else 1.0))

    emit_fm_matmul(p, qkv_scope, st, "qk", list(range(16)), NCH,
                   lambda d, oc: w_sb[d][:, oc*128:(oc+1)*128],
                   lambda d, blk: qkv_src[d][:, cols(blk)], evac_qk)

    def evac_v(tt, vb, bank, s_mm, mmcnt):
        return ev_qk(bank[:], s_mm, mmcnt, o_v[tt, :, vb*TB:(vb+1)*TB])

    s_ev_v, n_ev_v = emit_fm_matmul(p, qkv_scope, st, "v", list(range(NTILE)), NCH,
                   lambda d, tt: qkv_src[d][:, tt*128:(tt+1)*128],
                   lambda d, vb: w_sb[d][:, 2048+vb*TB:2048+(vb+1)*TB],
                   evac_v, nblk=2)
    ev_qk.flush_wait()
    p.t.wait_ge(s_ev_v, n_ev_v)
    p.v.wait_ge(ev_qk.s_dma, 16 * ev_qk.g)
    p.s.wait_ge(ev_qk.s_dma, 16 * ev_qk.g)
    qkv_scope.close()

    gscope = ExitStack()
    ps_g = [ps(gscope, nc, f"gps{i}", [128, NE]) for i in range(2)]
    ps_t = [ps(gscope, nc, f"gpt{i}", [128, 128]) for i in range(2)]
    s_gmm = sem(st, nc, "gmm")
    s_gv = sem(st, nc, "gv")
    s_gt = sem(st, nc, "gt")
    gateT_sb = sb(gscope, nc, "gateT", [NE, NTOK], F32)
    zeroNE = sb(gscope, nc, "zeroNE", [128, NE], F32)
    p.v.memset(zeroNE[:], 0.0)
    coef = [sb(gscope, nc, f"coef{i}", [128, NE], F32) for i in range(2)]
    work = {nm: sb(gscope, nc, f"gw_{nm}", [128, NE], F32)
            for nm in ("e", "pp", "pm", "ge2", "pw")}
    colw = {nm: sb(gscope, nc, f"gc_{nm}", [128, 1], F32)
            for nm in ("m", "negm", "sum", "rsum", "m1", "m2", "den", "rden")}
    VPT = 3 if layer == 0 else 1
    for tt in range(NTILE):
        bank = ps_g[tt % 2]
        if tt >= 2:
            p.t.wait_ge(s_gv, VPT * (tt - 2) + VPT)
        for d in range(NCH):
            h = p.t.matmul(bank[:], xn_sb[d][:, tt*128:(tt+1)*128],
                           gw_sb[d][:], start=(d == 0), stop=(d == NCH - 1))
        h.then_inc(s_gmm, 1)
        cf = coef[tt % 2]
        p.v.wait_ge(s_gmm, 2 * tt + 1)
        if layer == 0:
            p.v.reduce_max(colw["m"][:], bank[:], axis=mybir.AxisListType.X)
            p.v.tensor_scalar(colw["negm"][:], colw["m"][:], -1.0, None, OP.mult)\
                .then_inc(s_gv, 1)
            p.s.wait_ge(s_gv, VPT * tt + 1)
            p.s.activation(work["e"][:], bank[:], AF.Exp, bias=colw["negm"][:])\
                .then_inc(s_gv, 1)
            p.v.wait_ge(s_gv, VPT * tt + 2)
            p.v.reduce_sum(colw["sum"][:], work["e"][:], axis=mybir.AxisListType.X)
            p.v.reciprocal(colw["rsum"][:], colw["sum"][:])
            p.v.tensor_scalar(work["pp"][:], work["e"][:], colw["rsum"][:], None, OP.mult)
            p.v.reduce_max(colw["m1"][:], work["pp"][:], axis=mybir.AxisListType.X)
            p.v.tensor_scalar(work["ge2"][:], work["pp"][:], colw["m1"][:], None, OP.is_ge)
            p.v.tensor_tensor(work["pm"][:], work["pp"][:], work["ge2"][:], OP.mult)
            p.v.tensor_tensor(work["pm"][:], work["pp"][:], work["pm"][:], OP.subtract)
            p.v.reduce_max(colw["m2"][:], work["pm"][:], axis=mybir.AxisListType.X)
            p.v.tensor_tensor(colw["den"][:], colw["m1"][:], colw["m2"][:], OP.add)
            p.v.reciprocal(colw["rden"][:], colw["den"][:])
            p.v.tensor_scalar(work["ge2"][:], work["pp"][:], colw["m2"][:], None, OP.is_ge)
            p.v.tensor_scalar(work["pw"][:], work["pp"][:], colw["rden"][:], None, OP.mult)
            p.v.tensor_tensor(cf[:], work["pw"][:], work["ge2"][:], OP.mult)\
                .then_inc(s_gv, 1)
        else:
            p.v.tensor_copy(cf[:], bank[:]).then_inc(s_gv, 1)
        tbk = ps_t[tt % 2]
        p.t.wait_ge(s_gv, VPT * tt + VPT)
        if tt >= 2:
            p.t.wait_ge(s_gt, tt - 1)
        p.t.transpose(tbk[:NE, :], cf[:], id_sb[:]).then_inc(s_gmm, 1)
        p.v.wait_ge(s_gmm, 2 * tt + 2)
        if layer == 0:
            hh = p.v.tensor_copy(gateT_sb[:, tt*128:(tt+1)*128], tbk[:NE, :])
        else:
            hh = p.v.tensor_scalar(gateT_sb[:, tt*128:(tt+1)*128], tbk[:NE, :],
                                   bg_sb[:], None, OP.add)
        hh.then_inc(s_gt, 1)
    p.y.wait_ge(s_gt, NTILE)
    p.y.dma_start(o_gate, gateT_sb[:]).then_inc(s_fin, 16)
    fin += 16

    p.y.wait_ge(s_fin, fin)
    p.build()
    gscope.close()
    qkv_scope.close()
    st.close()
    return nc


def host_prep_inputs(layer, d, core):
    half = core % 2
    b = core // 2
    idx = own_token_idx(half)
    if layer == 0:
        x = d["L0_attn_x"][b][idx]
        wq = d["L0_attn_w_qkv"]
        wqkvT = np.ascontiguousarray(wq.T).reshape(NCH, 128, 3072)
        gwT = np.ascontiguousarray(d["L0_ffn_gate_w"].T).reshape(NCH, 128, 14)
        m = {
            "xT": np.ascontiguousarray(x.T).reshape(NCH, 128, NTOK),
            "wqkvT": wqkvT,
            "gwT": gwT,
            "ident": np.eye(128, dtype=np.float32),
            "gcol": d["L0_attn_ln_gamma"].reshape(NCH, 128, 1).astype(np.float32),
            "bcol": d["L0_attn_ln_beta"].reshape(NCH, 128, 1).astype(np.float32),
        }
    else:
        x = d["x1"][b][idx]
        wqkvT = np.ascontiguousarray(d["L1_attn_W_qkv"]).reshape(NCH, 128, 3072)
        gwT = np.ascontiguousarray(d["L1_ffn_Wg"]).reshape(NCH, 128, 16)
        m = {
            "xT": np.ascontiguousarray(x.T).reshape(NCH, 128, NTOK),
            "wqkvT": wqkvT,
            "gwT": gwT,
            "ident": np.eye(128, dtype=np.float32),
            "bgc": d["L1_ffn_bg"].reshape(16, 1).astype(np.float32),
        }
    return m


def attn_schedule(alibi):
    for J in range(4):
        for h in range(16):
            qb = qblk_of_head(h, alibi)
            for s_ in range(512 // qb):
                off = J * 512 + s_ * qb
                j0 = off // 128
                QB = max(qb // 128, 1)
                KT = 2 * (j0 + QB)
                NM = 2 * QB
                yield (J, h, qb, j0, off, KT, NM)

def qblk_of_head(h, alibi):
    if not alibi:
        return 512
    return {0: 64, 1: 128, 2: 128, 3: 128, 4: 128, 5: 256, 6: 256}.get(h, 512)

MASK_CLASSES_L0 = ((512, 8, [0], [0]),)
MASK_CLASSES_L1 = ((512, 8, [0], [0]), (256, 4, [0], [0]),
                   (128, 2, [0, 1], [0]), (64, 2, [0, 1], [0, 1]))

def mask_patterns(half, alibi):
    pats = {}
    def par_fn(lt):
        return (lt % 2) if half == 0 else (1 - lt % 2)
    for qb, rels, pars, h64s in (MASK_CLASSES_L1 if alibi else MASK_CLASSES_L0):
        QB = max(qb // 128, 1)
        for par in pars:
            for h64 in h64s:
                j0 = par
                for rel in range(rels):
                    m = np.zeros((128, qb), np.float32)
                    for c in range(qb):
                        lt = j0 + (c + (h64 * 64 if qb == 64 else 0)) // 128
                        a = 2 * lt + par_fn(lt)
                        qpos = a * 128 + (c + h64 * 64) % 128
                        kbase = (2 * j0 + rel) * 128
                        i = np.arange(128)
                        m[:, c] = (kbase + i <= qpos).astype(np.float32)
                    pats[(qb, rel, par, h64)] = m
    keys = sorted(pats.keys())
    offs, bank, o = {}, [], 0
    for k in keys:
        offs[k] = o
        bank.append(pats[k])
        o += k[0]
    return np.concatenate(bank, axis=1), offs

def mask_offsets(alibi):
    offs, o = {}, 0
    for k in sorted([(qb, rel, par, h64)
                     for qb, rels, pars, h64s in (MASK_CLASSES_L1 if alibi else MASK_CLASSES_L0)
                     for par in pars for h64 in h64s for rel in range(rels)]):
        offs[k] = o
        o += k[0]
    return offs, o

def host_attn_inputs(layer, core, q, k_full, vpad, woutT, boutcol):
    half = core % 2
    alibi = layer == 1
    tm = tmap(half)
    maskbank, _ = mask_patterns(half, alibi)
    ncols = sum(KT for (_, _, _, _, _, KT, _) in attn_schedule(alibi))
    biascols = np.zeros((ncols, 128), np.float32)
    i = 0
    for (J, h, qb, j0, off, KT, NM) in attn_schedule(alibi):
        lts = range(j0, j0 + max(qb // 128, 1))
        if qb == 64:
            a = 2 * j0 + ((j0 % 2) if half == 0 else (1 - j0 % 2))
            h64 = (off % 128) // 64
            maxpos = a * 128 + h64 * 64 + 63
        else:
            maxpos = max(2 * lt + (((lt % 2) if half == 0 else (1 - lt % 2)))
                         for lt in lts) * 128 + 127
        q_end = maxpos + 1
        own_max_tile = maxpos // 128
        sl = ALIBI_SLOPES[h] if alibi else 0.0
        for kt in range(KT):
            if kt > own_max_tile:
                biascols[i] = -60000.0
            elif alibi:
                biascols[i] = sl * ((kt * 128 + np.arange(128)) - q_end)
            i += 1
    import ml_dtypes
    m = {
        "q": q, "k": k_full, "vpad": vpad, "woutT": woutT, "boutcol": boutcol,
        "maskbank": np.ascontiguousarray(maskbank).astype(ml_dtypes.bfloat16),
    }
    if alibi:
        m["biascols"] = np.ascontiguousarray(biascols.T)
    return m


def build_attn(layer):
    alibi = layer == 1
    nc = bass.Bass("TRN2", target_bir_lowering=False)
    moffs, mcols = mask_offsets(alibi)
    ncols = sum(KT for (_, _, _, _, _, KT, _) in attn_schedule(alibi))
    q_h = nc.dram_tensor("q", [NCH, 128, NTOK], BF16, kind="ExternalInput").ap()
    k_h = nc.dram_tensor("k", [NCH, 128, 4096], BF16, kind="ExternalInput").ap()
    v_h = nc.dram_tensor("vpad", [32, 128, 1040], BF16, kind="ExternalInput").ap()
    w_h = nc.dram_tensor("woutT", [NCH, 128, D], F32, kind="ExternalInput").ap()
    bo_h = nc.dram_tensor("boutcol", [NCH, 128, 1], F32, kind="ExternalInput").ap()
    if alibi:
        bc_h = nc.dram_tensor("biascols", [128, ncols], F32, kind="ExternalInput").ap()
    mk_h = nc.dram_tensor("maskbank", [128, mcols], BF16, kind="ExternalInput").ap()
    o_s = nc.dram_tensor("o_s", [NCH, 128, NTOK], F32, kind="ExternalOutput").ap()

    p = Plan(nc)
    st = ExitStack()
    qj_sb = [[sb(st, nc, f"q{i}_{d}", [128, TB], BF16) for d in range(NCH)]
             for i in range(2)]
    k_sb = [sb(st, nc, f"k{d}", [128, 4096], BF16) for d in range(NCH)]
    v_sb = [sb(st, nc, f"v{t}", [128, 1040], BF16) for t in range(32)]
    w_sb = [sb(st, nc, f"w{d}", [128, D], BF16) for d in range(NCH)]
    bo_sb = [sb(st, nc, f"bo{d}", [128, 1], F32) for d in range(NCH)]
    bc_sb = sb(st, nc, "bc", [128, ncols], F32) if alibi else None
    mk_sb = sb(st, nc, "mk", [128, mcols], BF16)
    ones_f = sb(st, nc, "onesf", [1, 64], F32)
    abuf = [sb(st, nc, f"ab{i}", [128, TB], BF16) for i in range(4)]
    rrow = sb(st, nc, "rrow", [1, TB], F32)
    rb_sb = sb(st, nc, "rb", [64, TB], F32)
    s_buf = [sb(st, nc, f"sb{d}", [128, TB], BF16) for d in range(NCH)]

    s_ld = sem(st, nc, "ld")
    s_sc = sem(st, nc, "sc")
    s_exp = sem(st, nc, "exp")
    s_msk = sem(st, nc, "msk")
    s_av = sem(st, nc, "av")
    s_fz = sem(st, nc, "fz")
    s_pj = sem(st, nc, "pj")
    s_fin = sem(st, nc, "fin")

    s_qld = sem(st, nc, "qld")
    qld = [0]
    def load_qJ(Jn):
        for d in range(NCH):
            p.y.dma_start(qj_sb[Jn % 2][d][:], q_h[d, :, Jn*TB:(Jn+1)*TB])\
                .then_inc(s_qld, 16)
            qld[0] += 16
    ld = 0
    load_qJ(0)
    load_qJ(1)
    for d in range(NCH):
        p.y.dma_start(k_sb[d][:], k_h[d]).then_inc(s_ld, 16); ld += 16
    for t in range(32):
        p.y.dma_start(v_sb[t][:], v_h[t]).then_inc(s_ld, 16); ld += 16
    for d in range(NCH):
        p.g.dma_start(w_sb[d][:], w_h[d]).then_inc(s_ld, 16); ld += 16
        p.g.dma_start(bo_sb[d][:], bo_h[d]).then_inc(s_ld, 16); ld += 16
    if alibi:
        p.g.dma_start(bc_sb[:], bc_h).then_inc(s_ld, 16); ld += 16
    p.g.dma_start(mk_sb[:], mk_h).then_inc(s_ld, 16); ld += 16
    p.v.memset(ones_f[:], 1.0)
    for e in (p.t, p.s, p.v):
        e.wait_ge(s_ld, ld)

    ps_sc = [ps(st, nc, f"psc{i}", [128, TB]) for i in range(3)]
    ps_o = ps(st, nc, "pso", [65, TB])
    ps_b = ps(st, nc, "psb", [64, TB])
    ps_pj = [ps(st, nc, f"pspj{i}", [128, TB]) for i in range(2)]
    ev = StagedEvac(p, st, st, "es", n_stages=2, dtype=F32)

    C = dict(sc=0, exp=0, msk=0, av=0, fz=0, pj=0, g=0)
    frees_pj = []
    colix = 0
    sched = list(attn_schedule(alibi))
    si = 0
    for J in range(4):
        p.t.wait_ge(s_qld, 128 * (J + 1))
        items = [it for it in sched if it[0] == J]
        for (J_, h, qb, j0, off, KT, NM) in items:
            hp, hh = h // 2, h % 2
            rows = slice(hh * 64, hh * 64 + 64)
            qcols = slice(off - J * 512, off - J * 512 + qb)
            for kt in range(KT):
                g = C["g"]
                bank = ps_sc[g % 3]
                if g >= 3:
                    p.t.wait_ge(s_exp, g - 2)
                p.t.matmul(bank[:, :qb], k_sb[hp][rows, kt*128:(kt+1)*128],
                           qj_sb[J % 2][hp][rows, qcols], start=True, stop=True)\
                    .then_inc(s_sc, 1)
                C["sc"] += 1
                ab = abuf[g % 4]
                p.s.wait_ge(s_sc, C["sc"])
                if g >= 4:
                    p.s.wait_ge(s_av, g - 3)
                bias = bc_sb[:, colix:colix+1] if alibi else 0.0
                p.s.activation(ab[:, :qb], bank[:, :qb], AF.Exp, bias=bias)\
                    .then_inc(s_exp, 1)
                C["exp"] += 1
                colix += 1
                masked = kt >= KT - NM
                if masked:
                    rel = kt - (KT - NM)
                    par = j0 % 2 if qb <= 128 else 0
                    h64 = (off % 128) // 64 if qb == 64 else 0
                    mo = moffs[(qb, rel, par, h64)]
                    p.v.wait_ge(s_exp, C["exp"])
                    p.v.tensor_tensor(ab[:, :qb], ab[:, :qb],
                                      mk_sb[:, mo:mo+qb], OP.mult)\
                        .then_inc(s_msk, 1)
                    C["msk"] += 1
                if masked:
                    p.t.wait_ge(s_msk, C["msk"])
                else:
                    p.t.wait_ge(s_exp, C["exp"])
                if kt == 0:
                    if si > 0:
                        p.t.wait_ge(s_fz, 3 * si)
                p.t.matmul(ps_o[:, :qb], v_sb[kt][:, h*65:(h+1)*65],
                           ab[:, :qb], start=(kt == 0), stop=(kt == KT - 1))\
                    .then_inc(s_av, 1)
                C["av"] += 1
                C["g"] += 1
            p.v.wait_ge(s_av, C["av"])
            p.v.reciprocal(rrow[:, :qb], ps_o[64:65, :qb]).then_inc(s_fz, 1)
            C["fz"] += 1
            p.t.wait_ge(s_fz, C["fz"])
            if si > 0:
                p.t.wait_ge(s_fz, 3 * si)
            p.t.matmul(ps_b[:, :qb], ones_f[:], rrow[:, :qb], start=True,
                       stop=True).then_inc(s_sc, 1)
            C["sc"] += 1
            p.v.wait_ge(s_sc, C["sc"])
            p.v.tensor_copy(rb_sb[:, :qb], ps_b[:, :qb]).then_inc(s_fz, 1)
            C["fz"] += 1
            scols = slice(off - J * 512, off - J * 512 + qb)
            p.v.tensor_tensor(s_buf[hp][rows, scols], ps_o[0:64, :qb],
                              rb_sb[:, :qb], OP.mult).then_inc(s_fz, 1)
            C["fz"] += 1
            si += 1
        p.t.wait_ge(s_fz, 3 * si)
        for oc in range(NCH):
            bank = ps_pj[oc % 2]
            gp = J * NCH + oc
            if gp >= 2:
                fs, fv = frees_pj[gp - 2]
                p.t.wait_ge(fs, fv)
            for e in range(NCH):
                hmm = p.t.matmul(bank[:], w_sb[e][:, oc*128:(oc+1)*128],
                                 s_buf[e][:], start=(e == 0), stop=(e == NCH - 1))
            hmm.then_inc(s_pj, 1)
            C["pj"] += 1
            frees_pj.append(ev(bank[:], s_pj, C["pj"], o_s[oc, :, J*512:(J+1)*512],
                               bias=bo_sb[oc][:]))
        p.v.wait_ge(s_pj, C["pj"])
        if J + 2 < 4:
            p.y.wait_ge(s_pj, C["pj"])
            load_qJ(J + 2)
    ev.flush_wait()
    p.build()
    st.close()
    return nc


def build_moe0(NEXP=14):
    nc = bass.Bass("TRN2", target_bir_lowering=False)
    NE = 14
    xnbf_h = nc.dram_tensor("xnbf", [NCH, 128, NTOK], BF16, kind="ExternalInput").ap()
    xn_h = nc.dram_tensor("xn", [NCH, 128, NTOK], F32, kind="ExternalInput").ap()
    cf_h = nc.dram_tensor("coeffT", [NE, NTOK], F32, kind="ExternalInput").ap()
    w1_h = nc.dram_tensor("w1", [NE, NCH, 128, 2048], F32, kind="ExternalInput").ap()
    b1_h = nc.dram_tensor("b1c", [128, NE * 16], F32, kind="ExternalInput").ap()
    w2_h = nc.dram_tensor("w2", [NE, 16, 128, D], F32, kind="ExternalInput").ap()
    b2_h = nc.dram_tensor("b2r", [NE, D], F32, kind="ExternalInput").ap()
    x_h = nc.dram_tensor("xT", [NCH, 128, NTOK], F32, kind="ExternalInput").ap()
    s_h = nc.dram_tensor("s0", [NCH, 128, NTOK], F32, kind="ExternalInput").ap()
    sel_h = nc.dram_tensor("sel", [14, 128 * 14], F32, kind="ExternalInput").ap()
    o_x1 = nc.dram_tensor("o_x1", [NCH, 128, NTOK], F32, kind="ExternalOutput").ap()

    p = Plan(nc)
    st = ExitStack()
    xnbf_sb = [sb(st, nc, f"xnbf{d}", [128, NTOK], BF16) for d in range(NCH)]
    gacc = [sb(st, nc, f"ga{d}", [128, NTOK], F32) for d in range(NCH)]
    cf_sb = sb(st, nc, "cfsb", [NE, NTOK], F32)
    b1_sb = sb(st, nc, "b1sb", [128, NE * 16], F32)
    b2_sb = sb(st, nc, "b2sb", [NE, D], F32)
    ones_f = sb(st, nc, "onesf", [1, 128], F32)
    sel_sb = sb(st, nc, "selsb", [14, 128 * 14], F32)
    cb_sb = sb(st, nc, "cb", [128, TB], F32)
    tmp_sb = sb(st, nc, "tmp", [128, TB], F32)
    s_ld = sem(st, nc, "ld")
    s_w = sem(st, nc, "w")
    s_mm = sem(st, nc, "mm")
    s_ev = sem(st, nc, "ev")
    s_acc = sem(st, nc, "acc")
    s_cb = sem(st, nc, "cbk")
    s_fin = sem(st, nc, "fin")
    p.v.memset(ones_f[:], 1.0)

    ld = 0
    for d in range(NCH):
        p.y.dma_start(xnbf_sb[d][:], xnbf_h[d]).then_inc(s_ld, 16); ld += 16
    p.y.dma_start(cf_sb[:], cf_h).then_inc(s_ld, 16); ld += 16
    p.y.dma_start(b1_sb[:], b1_h).then_inc(s_ld, 16); ld += 16
    p.y.dma_start(b2_sb[:], b2_h).then_inc(s_ld, 16); ld += 16
    p.y.dma_start(sel_sb[:], sel_h).then_inc(s_ld, 16); ld += 16
    for e in (p.t, p.s, p.v):
        e.wait_ge(s_ld, ld)

    iscope = ExitStack()
    xnb = [sb(iscope, nc, f"xnb{i}", [128, NTOK], F32, side="right") for i in range(2)]
    with ExitStack() as sti:
        ps_i = [ps(sti, nc, f"pi{i}", [128, TB]) for i in range(4)]
        g = 0
        for d in range(NCH):
            if d >= 2:
                p.y.wait_ge(s_acc, (d - 1) * NBLK)
            p.y.dma_start(xnb[d % 2][:], xn_h[d]).then_inc(s_ld, 16)
            ld += 16
            p.v.wait_ge(s_ld, ld)
            for blk in range(NBLK):
                bank = ps_i[g % 4]
                if g >= 4:
                    p.t.wait_ge(s_acc, g - 3)
                p.t.matmul(bank[:], b2_sb[:, d*128:(d+1)*128], cf_sb[:, cols(blk)],
                           start=True, stop=True).then_inc(s_mm, 1)
                p.v.wait_ge(s_mm, g + 1)
                p.v.tensor_tensor(gacc[d][:, cols(blk)], xnb[d % 2][:, cols(blk)],
                                  bank[:], OP.add).then_inc(s_acc, 1)
                g += 1
        NB_INIT = g

    iscope.close()
    wscope = ExitStack()
    w1_sb = [sb(wscope, nc, f"w1_{d}", [128, 2048], BF16) for d in range(NCH)]
    w2_sb = [sb(wscope, nc, f"w2_{f}", [128, D], BF16) for f in range(16)]
    h_sb = [sb(wscope, nc, f"h{f}", [128, TB], BF16) for f in range(16)]
    ps_h = [ps(wscope, nc, f"ph{i}", [128, TB]) for i in range(2)]
    ps_o = [ps(wscope, nc, f"po{i}", [128, TB]) for i in range(2)]
    ps_c = ps(wscope, nc, "pc", [128, TB])
    wld = [0]
    ghc = [0]
    omc = [0]
    acc2 = [NB_INIT]
    cbc = [0]
    mmc = [NB_INIT]

    def load_we(e):
        p.g.wait_ge(s_acc, NB_INIT + e * 32)
        for d in range(NCH):
            p.g.dma_start(w1_sb[d][:], w1_h[e, d]).then_inc(s_w, 16); wld[0] += 16
        for f in range(16):
            p.g.dma_start(w2_sb[f][:], w2_h[e, f]).then_inc(s_w, 16); wld[0] += 16

    for e in range(NEXP):
        load_we(e)
        p.t.wait_ge(s_w, wld[0])
        for tb in range(NBLK):
            if cbc[0] > 0:
                p.t.wait_ge(s_cb, 2 * cbc[0])
            p.t.matmul(ps_c[:], sel_sb[:, e*128:(e+1)*128], cf_sb[:, cols(tb)],
                       start=True, stop=True).then_inc(s_mm, 1)
            mmc[0] += 1
            p.v.wait_ge(s_mm, mmc[0])
            p.v.tensor_copy(cb_sb[:], ps_c[:]).then_inc(s_cb, 1)
            for fc in range(16):
                bank = ps_h[fc % 2]
                gh = ghc[0]
                if gh >= 2:
                    p.t.wait_ge(s_ev, gh - 1)
                for d in range(NCH):
                    h = p.t.matmul(bank[:], w1_sb[d][:, fc*128:(fc+1)*128],
                                   xnbf_sb[d][:, cols(tb)],
                                   start=(d == 0), stop=(d == NCH - 1))
                h.then_inc(s_mm, 1)
                mmc[0] += 1
                p.s.wait_ge(s_mm, mmc[0])
                p.s.activation(h_sb[fc][:], bank[:], AF.Gelu,
                               bias=b1_sb[:, e*16+fc:e*16+fc+1]).then_inc(s_ev, 1)
                ghc[0] += 1
            for dd in range(NCH):
                bank = ps_o[dd % 2]
                om = omc[0]
                if dd == 0:
                    p.t.wait_ge(s_ev, ghc[0])
                if om >= 2:
                    p.t.wait_ge(s_acc, acc2[0] - 1)
                for f in range(16):
                    h = p.t.matmul(bank[:], w2_sb[f][:, dd*128:(dd+1)*128],
                                   h_sb[f][:], start=(f == 0), stop=(f == 15))
                h.then_inc(s_mm, 1)
                mmc[0] += 1
                p.v.wait_ge(s_mm, mmc[0])
                p.v.tensor_tensor(tmp_sb[:], bank[:], cb_sb[:], OP.mult)
                p.v.tensor_tensor(gacc[dd][:, cols(tb)], gacc[dd][:, cols(tb)],
                                  tmp_sb[:], OP.add).then_inc(s_acc, 1)
                acc2[0] += 1
                omc[0] += 1
            p.v.nop().then_inc(s_cb, 1)
            cbc[0] += 1
    wscope.close()

    fscope = ExitStack()
    xb = [sb(fscope, nc, f"xb{i}", [128, NTOK], F32, side="right") for i in range(2)]
    sb_ = [sb(fscope, nc, f"sbb{i}", [128, NTOK], F32, side="right") for i in range(2)]
    gb = sb(fscope, nc, "gb", [128, NTOK], F32, side="right")
    x1b = [sb(fscope, nc, f"x1b{i}", [128, NTOK], F32, side="right") for i in range(2)]
    s_clip = sem(st, nc, "clip")
    s_silu = sem(st, nc, "silu")
    s_x1 = sem(st, nc, "x1")
    s_od = sem(st, nc, "od")
    echo = sb(fscope, nc, "echo", [128, 1], F32, side="right")
    p.y.wait_ge(s_acc, acc2[0])
    for d in range(NCH):
        if d >= 2:
            p.y.wait_ge(s_x1, d - 1)
        p.y.dma_start(xb[d % 2][:], x_h[d]).then_inc(s_ld, 16)
        p.y.dma_start(sb_[d % 2][:], s_h[d]).then_inc(s_ld, 16)
        ld += 32
        p.v.wait_ge(s_ld, ld)
        p.v.tensor_scalar(gacc[d][:], gacc[d][:], 100.0, -100.0, OP.min, OP.max)\
            .then_inc(s_clip, 1)
        p.s.wait_ge(s_clip, d + 1)
        if d >= 1:
            p.s.wait_ge(s_x1, d)
        p.s.activation(gb[:], gacc[d][:], AF.Silu).then_inc(s_silu, 1)
        p.v.wait_ge(s_silu, d + 1)
        if d >= 2:
            p.v.wait_ge(s_od, 16 * (d - 1))
        p.v.tensor_tensor(gb[:], gb[:], sb_[d % 2][:], OP.mult)
        p.v.tensor_tensor(x1b[d % 2][:], gb[:], xb[d % 2][:], OP.add)
        p.v.tensor_copy(echo[:], x1b[d % 2][:, NTOK-1:NTOK]).then_inc(s_x1, 1)
        p.y.wait_ge(s_x1, d + 1)
        p.y.dma_start(o_x1[d], x1b[d % 2][:]).then_inc(s_od, 16)
    p.y.wait_ge(s_od, 16 * NCH)
    p.build()
    fscope.close()
    st.close()
    return nc


def host_moe0_inputs(d, core, xn, xnbf, coeffT, s0, ri=None):
    half = core % 2
    b = core // 2
    return {
        "xnbf": xnbf, "xn": xn, "coeffT": coeffT,
        "w1": np.ascontiguousarray(d["L0_ffn_w1"].reshape(14, 8, 128, 2048)),
        "b1c": np.ascontiguousarray(d["L0_ffn_b1"].reshape(14 * 16, 128).T),
        "w2": np.ascontiguousarray(d["L0_ffn_w2"].reshape(14, 16, 128, 1024)),
        "b2r": np.ascontiguousarray(d["L0_ffn_b2"]),
        "xT": np.ascontiguousarray(d["L0_attn_x"][b][own_token_idx(half)].T)
              .reshape(NCH, 128, NTOK),
        "s0": s0,
        "sel": np.ascontiguousarray(
            np.repeat(np.eye(14, dtype=np.float32), 128, axis=1)),
    }


def build_ffn1(NEXP=16, BIS=40):
    nc = bass.Bass("TRN2", target_bir_lowering=False)
    NE = 16
    S = 4096
    xnbf_h = nc.dram_tensor("xnbf", [NCH, 128, NTOK], BF16, kind="ExternalInput").ap()
    lg_h = nc.dram_tensor("logitsP", [NE, S], F32, kind="ExternalInput").ap()
    w1_h = nc.dram_tensor("w1", [NE, NCH, 128, D], F32, kind="ExternalInput").ap()
    w2_h = nc.dram_tensor("w2", [NE, NCH, 128, D], F32, kind="ExternalInput").ap()
    w3_h = nc.dram_tensor("w3", [NE, NCH, 128, D], F32, kind="ExternalInput").ap()
    b1_h = nc.dram_tensor("b1r", [1, NE], F32, kind="ExternalInput").ap()
    b2_h = nc.dram_tensor("b2r", [1, NE], F32, kind="ExternalInput").ap()
    b3_h = nc.dram_tensor("b3c", [NE, 1], F32, kind="ExternalInput").ap()
    sel_h = nc.dram_tensor("sel", [NE, 128 * NE], F32, kind="ExternalInput").ap()
    x_h = nc.dram_tensor("x1T", [NCH, 128, NTOK], F32, kind="ExternalInput").ap()
    s_h = nc.dram_tensor("s1", [NCH, 128, NTOK], F32, kind="ExternalInput").ap()
    o_x2 = nc.dram_tensor("o_x2", [NCH, 128, NTOK], F32, kind="ExternalOutput").ap()

    p = Plan(nc)
    st = ExitStack()
    xnbf_sb = [sb(st, nc, f"xnbf{d}", [128, NTOK], BF16) for d in range(NCH)]
    gacc = [sb(st, nc, f"ga{d}", [128, NTOK], F32) for d in range(NCH)]
    coefP = sb(st, nc, "coefP", [128, S], F32)
    sel_sb = sb(st, nc, "selsb", [NE, 128 * NE], F32)
    b3c_sb = sb(st, nc, "b3csb", [NE, 1], F32)
    ones_f = sb(st, nc, "onesf", [1, 128], F32)
    cb_sb = sb(st, nc, "cb", [128, TB], F32)
    tmp_sb = sb(st, nc, "tmp", [128, TB], F32)
    B1c = sb(st, nc, "B1c", [128, NE], F32)
    B2c = sb(st, nc, "B2c", [128, NE], F32)
    rowb = sb(st, nc, "rowb", [1, TB], F32)
    colw = {nm: sb(st, nc, f"c_{nm}", [128, 1], F32)
            for nm in ("mx", "negmx", "sum", "rsum", "lo", "hi", "u", "t",
                       "cnt", "sel1", "nsel", "dlt")}
    s_ld = sem(st, nc, "ld")
    s_w = sem(st, nc, "w")
    s_mm = sem(st, nc, "mm")
    s_ev = sem(st, nc, "ev")
    s_acc = sem(st, nc, "acc")
    s_cb = sem(st, nc, "cbk")
    s_gate = sem(st, nc, "gate")
    p.v.memset(ones_f[:], 1.0)

    ld = 0
    for d in range(NCH):
        p.y.dma_start(xnbf_sb[d][:], xnbf_h[d]).then_inc(s_ld, 16); ld += 16
    p.y.dma_start(sel_sb[:], sel_h).then_inc(s_ld, 16); ld += 16
    p.y.dma_start(b3c_sb[:], b3_h).then_inc(s_ld, 16); ld += 16

    gscope = ExitStack()
    lv_sb = sb(gscope, nc, "lv", [128, S], F32, side="right")
    pv_sb = sb(gscope, nc, "pv", [128, S], F32, side="right")
    ge_sb = sb(gscope, nc, "ge", [128, S], F32, side="right")
    b1r_sb = sb(gscope, nc, "b1rsb", [1, NE], F32, side="right")
    b2r_sb = sb(gscope, nc, "b2rsb", [1, NE], F32, side="right")
    p.v.memset(lv_sb[:], 0.0).then_inc(s_gate, 100)
    p.y.wait_ge(s_gate, 100)
    p.y.dma_start(lv_sb[0:NE, :], lg_h).then_inc(s_ld, 16); ld += 16
    p.y.dma_start(b1r_sb[:], b1_h).then_inc(s_ld, 16); ld += 16
    p.y.dma_start(b2r_sb[:], b2_h).then_inc(s_ld, 16); ld += 16
    for e in (p.t, p.s, p.v):
        e.wait_ge(s_ld, ld)
    with ExitStack() as stb:
        ps_bc = [ps(stb, nc, f"pbc{i}", [128, NE]) for i in range(2)]
        p.t.matmul(ps_bc[0][:], ones_f[:], b1r_sb[:], start=True, stop=True)
        p.t.matmul(ps_bc[1][:], ones_f[:], b2r_sb[:], start=True, stop=True)\
            .then_inc(s_mm, 1)
        p.v.wait_ge(s_mm, 1)
        p.v.tensor_copy(B1c[:], ps_bc[0][:])
        p.v.tensor_copy(B2c[:], ps_bc[1][:]).then_inc(s_gate, 1)
    X = mybir.AxisListType.X
    p.v.reduce_max(colw["mx"][:], lv_sb[:], axis=X)
    p.v.tensor_scalar(colw["negmx"][:], colw["mx"][:], -1.0, None, OP.mult)\
        .then_inc(s_gate, 1)
    p.s.wait_ge(s_gate, 102)
    p.s.activation(pv_sb[:], lv_sb[:], AF.Exp, bias=colw["negmx"][:])\
        .then_inc(s_gate, 1)
    p.v.wait_ge(s_gate, 103)
    p.v.reduce_sum(colw["sum"][:], pv_sb[:], axis=X)
    p.v.reciprocal(colw["rsum"][:], colw["sum"][:])
    p.v.tensor_scalar(pv_sb[:], pv_sb[:], colw["rsum"][:], None, OP.mult)
    p.v.memset(colw["lo"][:], 0.0)
    p.v.memset(colw["hi"][:], 1.0)
    for it in range(BIS):
        p.v.tensor_tensor(colw["u"][:], colw["lo"][:], colw["hi"][:], OP.add)
        p.v.tensor_scalar(colw["t"][:], colw["u"][:], 0.5, None, OP.mult)
        p.v.tensor_scalar(ge_sb[:], pv_sb[:], colw["t"][:], None, OP.is_ge)
        p.v.reduce_sum(colw["cnt"][:], ge_sb[:], axis=X)
        p.v.tensor_scalar(colw["sel1"][:], colw["cnt"][:], 256.0, None, OP.is_ge)
        p.v.tensor_scalar(colw["nsel"][:], colw["sel1"][:], -1.0, 1.0, OP.mult, OP.add)
        p.v.tensor_tensor(colw["dlt"][:], colw["t"][:], colw["lo"][:], OP.subtract)
        p.v.tensor_tensor(colw["dlt"][:], colw["dlt"][:], colw["sel1"][:], OP.mult)
        p.v.tensor_tensor(colw["lo"][:], colw["lo"][:], colw["dlt"][:], OP.add)
        p.v.tensor_tensor(colw["dlt"][:], colw["t"][:], colw["hi"][:], OP.subtract)
        p.v.tensor_tensor(colw["dlt"][:], colw["dlt"][:], colw["nsel"][:], OP.mult)
        p.v.tensor_tensor(colw["hi"][:], colw["hi"][:], colw["dlt"][:], OP.add)
    p.v.tensor_scalar(ge_sb[:], pv_sb[:], colw["lo"][:], None, OP.is_ge)
    p.v.tensor_tensor(coefP[:], pv_sb[:], ge_sb[:], OP.mult).then_inc(s_gate, 1)

    with ExitStack() as sti:
        ps_r = ps(sti, nc, "pr", [1, TB])
        ps_b = [ps(sti, nc, f"pb{i}", [128, TB]) for i in range(2)]
        p.t.wait_ge(s_gate, 104)
        g = 0
        acc0 = 0
        for tb in range(NBLK):
            if tb >= 1:
                p.t.wait_ge(s_gate, 104 + tb)
            p.t.matmul(ps_r[:], b3c_sb[:], coefP[0:NE, cols(tb)], start=True,
                       stop=True).then_inc(s_mm, 1)
            p.v.wait_ge(s_mm, 2 + 3 * tb)
            p.v.tensor_copy(rowb[:], ps_r[:]).then_inc(s_gate, 1)
            bank = ps_b[tb % 2]
            p.t.wait_ge(s_gate, 105 + tb)
            if tb >= 2:
                p.t.wait_ge(s_acc, (tb - 1) * NCH)
            p.t.matmul(bank[:], ones_f[:], rowb[:], start=True, stop=True)\
                .then_inc(s_mm, 2)
            p.v.wait_ge(s_mm, 4 + 3 * tb)
            for d in range(NCH):
                p.v.tensor_copy(gacc[d][:, cols(tb)], bank[:]).then_inc(s_acc, 1)
                acc0 += 1
        NB_INIT = acc0
        MM_INIT = 1 + 3 * NBLK
    gscope.close()

    wscope = ExitStack()
    w1_sb = [sb(wscope, nc, f"w1_{d}", [128, D], BF16) for d in range(NCH)]
    w2_sb = [sb(wscope, nc, f"w2_{d}", [128, D], BF16) for d in range(NCH)]
    w3_sb = [sb(wscope, nc, f"w3_{d}", [128, D], BF16) for d in range(NCH)]
    h1 = [sb(wscope, nc, f"h1_{d}", [128, TB], BF16) for d in range(NCH)]
    h2 = [sb(wscope, nc, f"h2_{d}", [128, TB], BF16) for d in range(NCH)]
    ps_h = [ps(wscope, nc, f"ph{i}", [128, TB]) for i in range(2)]
    ps_o = [ps(wscope, nc, f"po{i}", [128, TB]) for i in range(2)]
    ps_c = ps(wscope, nc, "pc", [128, TB])
    wld = [0]
    evc = [0]
    omc = [0]
    accc = [NB_INIT]
    cbc = [0]
    mmc = [MM_INIT]

    for e in range(NEXP):
        p.g.wait_ge(s_acc, NB_INIT + e * 32)
        for d in range(NCH):
            p.g.dma_start(w1_sb[d][:], w1_h[e, d]).then_inc(s_w, 16); wld[0] += 16
            p.g.dma_start(w2_sb[d][:], w2_h[e, d]).then_inc(s_w, 16); wld[0] += 16
            p.g.dma_start(w3_sb[d][:], w3_h[e, d]).then_inc(s_w, 16); wld[0] += 16
        p.t.wait_ge(s_w, wld[0])
        for tb in range(NBLK):
            if cbc[0] > 0:
                p.t.wait_ge(s_cb, 2 * cbc[0])
            p.t.matmul(ps_c[:], sel_sb[:, e*128:(e+1)*128], coefP[0:NE, cols(tb)],
                       start=True, stop=True).then_inc(s_mm, 1)
            mmc[0] += 1
            p.v.wait_ge(s_mm, mmc[0])
            p.v.tensor_copy(cb_sb[:], ps_c[:]).then_inc(s_cb, 1)
            for stage, wbuf, src, dst, bcol in (
                    (1, w1_sb, None, h1, B1c), (2, w2_sb, h1, h2, B2c)):
                for oc in range(NCH):
                    bank = ps_h[oc % 2]
                    gh = evc[0]
                    if stage == 2 and oc == 0:
                        p.t.wait_ge(s_ev, evc[0])
                    elif gh >= 2:
                        p.t.wait_ge(s_ev, gh - 1)
                    for d in range(NCH):
                        rhs = xnbf_sb[d][:, cols(tb)] if stage == 1 else src[d][:]
                        h = p.t.matmul(bank[:], wbuf[d][:, oc*128:(oc+1)*128],
                                       rhs, start=(d == 0), stop=(d == NCH - 1))
                    h.then_inc(s_mm, 1)
                    mmc[0] += 1
                    p.s.wait_ge(s_mm, mmc[0])
                    p.s.activation(dst[oc][:], bank[:], AF.Relu,
                                   bias=bcol[:, e:e+1]).then_inc(s_ev, 1)
                    evc[0] += 1
            for dd in range(NCH):
                bank = ps_o[dd % 2]
                om = omc[0]
                if dd == 0:
                    p.t.wait_ge(s_ev, evc[0])
                if om >= 2:
                    p.t.wait_ge(s_acc, accc[0] - 1)
                for f in range(NCH):
                    h = p.t.matmul(bank[:], w3_sb[f][:, dd*128:(dd+1)*128],
                                   h2[f][:], start=(f == 0), stop=(f == NCH - 1))
                h.then_inc(s_mm, 1)
                mmc[0] += 1
                p.v.wait_ge(s_mm, mmc[0])
                p.v.tensor_tensor(tmp_sb[:], bank[:], cb_sb[:], OP.mult)
                p.v.tensor_tensor(gacc[dd][:, cols(tb)], gacc[dd][:, cols(tb)],
                                  tmp_sb[:], OP.add).then_inc(s_acc, 1)
                accc[0] += 1
                omc[0] += 1
            p.v.nop().then_inc(s_cb, 1)
            cbc[0] += 1
    wscope.close()

    fscope = ExitStack()
    xb = [sb(fscope, nc, f"xb{i}", [128, NTOK], F32, side="right") for i in range(2)]
    sb_ = [sb(fscope, nc, f"sbb{i}", [128, NTOK], F32, side="right") for i in range(2)]
    gb = sb(fscope, nc, "gb", [128, NTOK], F32, side="right")
    x1b = [sb(fscope, nc, f"x1b{i}", [128, NTOK], F32, side="right") for i in range(2)]
    echo = sb(fscope, nc, "echo", [128, 1], F32, side="right")
    s_silu = sem(st, nc, "silu")
    s_x1 = sem(st, nc, "x1")
    s_od = sem(st, nc, "od")
    p.y.wait_ge(s_acc, accc[0])
    for d in range(NCH):
        if d >= 2:
            p.y.wait_ge(s_x1, d - 1)
        p.y.dma_start(xb[d % 2][:], x_h[d]).then_inc(s_ld, 16)
        p.y.dma_start(sb_[d % 2][:], s_h[d]).then_inc(s_ld, 16)
        ld += 32
        p.s.wait_ge(s_ld, ld)
        if d >= 1:
            p.s.wait_ge(s_x1, d)
        p.s.activation(gb[:], gacc[d][:], AF.Silu).then_inc(s_silu, 1)
        p.v.wait_ge(s_silu, d + 1)
        p.v.wait_ge(s_ld, ld)
        if d >= 2:
            p.v.wait_ge(s_od, 16 * (d - 1))
        p.v.tensor_tensor(gb[:], gb[:], sb_[d % 2][:], OP.mult)
        p.v.tensor_tensor(x1b[d % 2][:], gb[:], xb[d % 2][:], OP.add)
        p.v.tensor_copy(echo[:], x1b[d % 2][:, NTOK-1:NTOK]).then_inc(s_x1, 1)
        p.y.wait_ge(s_x1, d + 1)
        p.y.dma_start(o_x2[d], x1b[d % 2][:]).then_inc(s_od, 16)
    p.y.wait_ge(s_od, 16 * NCH)
    p.build()
    fscope.close()
    st.close()
    return nc


def host_ffn1_inputs(d, core, xnbf, logitsP, x1T, s1):
    return {
        "xnbf": xnbf, "logitsP": logitsP,
        "w1": np.ascontiguousarray(d["L1_ffn_w1"].reshape(16, 8, 128, 1024)),
        "w2": np.ascontiguousarray(d["L1_ffn_w2"].reshape(16, 8, 128, 1024)),
        "w3": np.ascontiguousarray(d["L1_ffn_w3"].reshape(16, 8, 128, 1024)),
        "b1r": np.ascontiguousarray(d["L1_ffn_b1"].reshape(1, 16)),
        "b2r": np.ascontiguousarray(d["L1_ffn_b2"].reshape(1, 16)),
        "b3c": np.ascontiguousarray(d["L1_ffn_b3"].reshape(16, 1)),
        "sel": np.ascontiguousarray(
            np.repeat(np.eye(16, dtype=np.float32), 128, axis=1)),
        "x1T": x1T, "s1": s1,
    }


from concourse.bass_utils import run_bass_kernel_spmd
import ml_dtypes


def _assemble_kv(res_list):
    idx = [own_token_idx(0), own_token_idx(1)]
    out = []
    for b in range(4):
        kf = np.empty((NCH, 128, 4096), dtype=ml_dtypes.bfloat16)
        for half in (0, 1):
            kf[:, :, idx[half]] = res_list[2 * b + half]["o_k"]
        vp = np.ones((32, 128, 16, 65), dtype=ml_dtypes.bfloat16)
        for half in (0, 1):
            ov = res_list[2 * b + half]["o_v"].reshape(NTILE, 128, 16, 64)
            for lt, t in enumerate(tmap(half)):
                vp[t, :, :, :64] = ov[lt]
        out.append((kf, vp.reshape(32, 128, 1040)))
    return out


def kernel(**d):
    cores = list(range(8))
    eye128 = np.eye(128, dtype=np.float32)

    wqkvT0 = np.ascontiguousarray(d["L0_attn_w_qkv"].T.astype(np.float32)).reshape(NCH, 128, 3072)
    gwT0 = np.ascontiguousarray(d["L0_ffn_gate_w"].T.astype(np.float32)).reshape(NCH, 128, 14)
    gcol = np.ascontiguousarray(d["L0_attn_ln_gamma"].astype(np.float32)).reshape(NCH, 128, 1)
    bcol = np.ascontiguousarray(d["L0_attn_ln_beta"].astype(np.float32)).reshape(NCH, 128, 1)
    x0 = np.asarray(d["L0_attn_x"], dtype=np.float32)
    in_maps = []
    for c in cores:
        b, half = c // 2, c % 2
        xT = np.ascontiguousarray(x0[b][own_token_idx(half)].T).reshape(NCH, 128, NTOK)
        in_maps.append({"xT": xT, "wqkvT": wqkvT0, "gwT": gwT0, "ident": eye128,
                        "gcol": gcol, "bcol": bcol})
    nc = build_prep(0)
    r1 = run_bass_kernel_spmd(nc, in_maps, core_ids=cores).results

    kv = _assemble_kv(r1)
    woutT0 = np.ascontiguousarray(d["L0_attn_w_out"].T.astype(np.float32)).reshape(NCH, 128, D)
    bout0 = np.ascontiguousarray(d["L0_attn_b_out"].astype(np.float32)).reshape(NCH, 128, 1)
    in_maps = []
    for c in cores:
        b = c // 2
        in_maps.append(host_attn_inputs(0, c, r1[c]["o_q"], kv[b][0], kv[b][1],
                                        woutT0, bout0))
    nc = build_attn(0)
    r2 = run_bass_kernel_spmd(nc, in_maps, core_ids=cores).results

    w1h = np.ascontiguousarray(d["L0_ffn_w1"].astype(np.float32)).reshape(14, 8, 128, 2048)
    b1h = np.ascontiguousarray(d["L0_ffn_b1"].astype(np.float32).reshape(14 * 16, 128).T)
    w2h = np.ascontiguousarray(d["L0_ffn_w2"].astype(np.float32)).reshape(14, 16, 128, 1024)
    b2h = np.ascontiguousarray(d["L0_ffn_b2"].astype(np.float32))
    sel14 = np.ascontiguousarray(np.repeat(np.eye(14, dtype=np.float32), 128, axis=1))
    in_maps = []
    for c in cores:
        b, half = c // 2, c % 2
        xT = np.ascontiguousarray(x0[b][own_token_idx(half)].T).reshape(NCH, 128, NTOK)
        in_maps.append({"xnbf": r1[c]["o_xnbf"], "xn": r1[c]["o_xn"],
                        "coeffT": r1[c]["o_gate"], "w1": w1h, "b1c": b1h,
                        "w2": w2h, "b2r": b2h, "xT": xT, "s0": r2[c]["o_s"],
                        "sel": sel14})
    nc = build_moe0()
    r3 = run_bass_kernel_spmd(nc, in_maps, core_ids=cores).results

    wqkvT1 = np.ascontiguousarray(d["L1_attn_W_qkv"].astype(np.float32)).reshape(NCH, 128, 3072)
    gwT1 = np.ascontiguousarray(d["L1_ffn_Wg"].astype(np.float32)).reshape(NCH, 128, 16)
    bgc = np.ascontiguousarray(d["L1_ffn_bg"].astype(np.float32)).reshape(16, 1)
    in_maps = []
    for c in cores:
        in_maps.append({"xT": np.ascontiguousarray(r3[c]["o_x1"]),
                        "wqkvT": wqkvT1, "gwT": gwT1, "ident": eye128, "bgc": bgc})
    nc = build_prep(1)
    r4 = run_bass_kernel_spmd(nc, in_maps, core_ids=cores).results

    kv = _assemble_kv(r4)
    woutT1 = np.ascontiguousarray(d["L1_attn_W_out"].astype(np.float32)).reshape(NCH, 128, D)
    bout1 = np.zeros((NCH, 128, 1), np.float32)
    in_maps = []
    for c in cores:
        b = c // 2
        in_maps.append(host_attn_inputs(1, c, r4[c]["o_q"], kv[b][0], kv[b][1],
                                        woutT1, bout1))
    nc = build_attn(1)
    r5 = run_bass_kernel_spmd(nc, in_maps, core_ids=cores).results

    w1f = np.ascontiguousarray(d["L1_ffn_w1"].astype(np.float32)).reshape(16, 8, 128, 1024)
    w2f = np.ascontiguousarray(d["L1_ffn_w2"].astype(np.float32)).reshape(16, 8, 128, 1024)
    w3f = np.ascontiguousarray(d["L1_ffn_w3"].astype(np.float32)).reshape(16, 8, 128, 1024)
    b1f = np.ascontiguousarray(d["L1_ffn_b1"].astype(np.float32).reshape(1, 16))
    b2f = np.ascontiguousarray(d["L1_ffn_b2"].astype(np.float32).reshape(1, 16))
    b3f = np.ascontiguousarray(d["L1_ffn_b3"].astype(np.float32).reshape(16, 1))
    sel16 = np.ascontiguousarray(np.repeat(np.eye(16, dtype=np.float32), 128, axis=1))
    in_maps = []
    for c in cores:
        b, half = c // 2, c % 2
        lgP = np.ascontiguousarray(np.concatenate(
            [r4[c]["o_gate"], r4[2 * b + (1 - half)]["o_gate"]], axis=1))
        in_maps.append({"xnbf": r4[c]["o_xnbf"], "logitsP": lgP,
                        "w1": w1f, "w2": w2f, "w3": w3f,
                        "b1r": b1f, "b2r": b2f, "b3c": b3f, "sel": sel16,
                        "x1T": np.ascontiguousarray(r3[c]["o_x1"]),
                        "s1": r5[c]["o_s"]})
    nc = build_ffn1()
    r6 = run_bass_kernel_spmd(nc, in_maps, core_ids=cores).results

    out = np.empty((4, 4096, 1024), np.float32)
    for c in cores:
        b, half = c // 2, c % 2
        out[b][own_token_idx(half)] = r6[c]["o_x2"].reshape(1024, NTOK).T
    return out
